# version 38
# baseline (speedup 1.0000x reference)
import numpy as np
import ml_dtypes
from contextlib import ExitStack

import concourse.bass as bass
import concourse.mybir as mybir
from concourse.bass_utils import run_bass_kernel_spmd

F32 = mybir.dt.float32
BF16 = mybir.dt.bfloat16
I8 = mybir.dt.int8
AF = mybir.ActivationFunctionType
ALU = mybir.AluOpType
AX = mybir.AxisListType

NCORES = 8
D = 2048
SEQ = 16384
TOK = SEQ // NCORES
HALO = 512
TOKP = TOK + HALO
NG = TOKP // 512
NCH = TOKP // 128
H, KH, HD = 16, 4, 128
INTER = 8192
NEG = -1e30

ENGINES = ("tensor", "vector", "scalar", "sync", "gpsimd")


class Sem:
    def __init__(self, handle):
        self.h = handle
        self.n = 0


class Prog:
    def __init__(self, nc, ctx):
        self.nc = nc
        self.ctx = ctx
        self.q = {e: [] for e in ENGINES}
        self.sched = {}

    def sem(self, name):
        return Sem(self.ctx.enter_context(self.nc.semaphore(name)))

    def op(self, eng, fn, waits=(), incs=()):
        waits = tuple((s, v) for s, v in waits)
        incs = tuple(incs)
        v0 = incs[0][0].n + incs[0][1] if incs else None

        def run(proxy, fn=fn, waits=waits, incs=incs, v0=v0):
            for s, v in waits:
                proxy.wait_ge(s.h, v)
            ins = fn(proxy)
            if incs:
                ins.then_inc(incs[0][0].h, incs[0][1])
            if len(incs) > 1:
                proxy.wait_ge(incs[0][0].h, v0)
                for s, k in incs[1:]:
                    proxy.nop().then_inc(s.h, k)
        self.q[eng].append(run)
        self.sched.setdefault(eng, []).append((waits, incs))
        for s, k in incs:
            s.n += k

    def check_deadlock(self):
        heads = {e: 0 for e in self.sched}
        vals = {}
        progressed = True
        while progressed:
            progressed = False
            for e, lst in self.sched.items():
                while heads[e] < len(lst):
                    waits, incs = lst[heads[e]]
                    if all(vals.get(id(s), 0) >= v for s, v in waits):
                        for s, k in incs:
                            vals[id(s)] = vals.get(id(s), 0) + k
                        heads[e] += 1
                        progressed = True
                    else:
                        break
        stuck = {e: (heads[e], len(lst)) for e, lst in self.sched.items()
                 if heads[e] < len(lst)}
        if not stuck:
            return None
        rep = [f"STUCK: {stuck}"]
        for e, lst in self.sched.items():
            if heads[e] < len(lst):
                waits, incs = lst[heads[e]]
                unmet = [(getattr(s.h, 'name', '?'), v, vals.get(id(s), 0))
                         for s, v in waits if vals.get(id(s), 0) < v]
                rep.append(f"  {e}[{heads[e]}] waits {unmet}")
        return "\n".join(rep)

    def flush(self, block):
        for e in ENGINES:
            lst = self.q[e]
            if not lst:
                continue

            def body(proxy, lst=lst):
                for f in lst:
                    f(proxy)
            getattr(block, e)(body)
        self.q = {e: [] for e in ENGINES}


def build(dbg=None, serq=False):
    nc = bass.Bass("TRN2", target_bir_lowering=False, debug=False)
    dt = nc.dram_tensor

    x_in = dt("x", [TOKP, D], F32, kind="ExternalInput").ap()
    wq_in = dt("wq", [128, 16, D], F32, kind="ExternalInput").ap()
    wk_in = dt("wk", [128, 16, 512], F32, kind="ExternalInput").ap()
    wv_in = dt("wv", [128, 16, 512], F32, kind="ExternalInput").ap()
    wo_in = dt("wo", [128, 16, D], F32, kind="ExternalInput").ap()
    wg_in = dt("wg", [128, 16, INTER], I8, kind="ExternalInput").ap()
    wu_in = dt("wu", [128, 16, INTER], I8, kind="ExternalInput").ap()
    wd_in = dt("wd", [64, 128, D], I8, kind="ExternalInput").ap()
    sg_in = dt("sg", [128, 64], F32, kind="ExternalInput").ap()
    su_in = dt("su", [128, 64], F32, kind="ExternalInput").ap()
    sdb_in = dt("sdb", [128, D], F32, kind="ExternalInput").ap()
    lnw_in = dt("lnw", [128, D], F32, kind="ExternalInput").ap()
    ropec_in = dt("ropec", [128, TOKP], F32, kind="ExternalInput").ap()
    ropes_in = dt("ropes", [128, TOKP], F32, kind="ExternalInput").ap()
    mkf_in = dt("maskf", [128, 4 * 640], BF16, kind="ExternalInput").ap()
    m0_in = dt("mask0", [128, 4 * 640], BF16, kind="ExternalInput").ap()
    psw_in = dt("pswap", [128, 128], F32, kind="ExternalInput").ap()
    idn_in = dt("ident", [128, 128], F32, kind="ExternalInput").ap()

    xn_dram = dt("xn_scr", [8, 128, 16 * 256], BF16, kind="Internal").ap()
    s_dram = dt("s_scr", [TOK, D], F32, kind="Internal").ap()
    y_out = dt("y", [TOK, D], F32, kind="ExternalOutput").ap()
    dbg_out = None
    if dbg == "kvq1":
        dbg_out = dt("dbgo", [128, 4 * TOKP + 20 * 512 + 16 * 512], F32,
                     kind="ExternalOutput").ap()
    elif dbg == "attn1":
        dbg_out = dt("dbgo", [128, 16 * 512], F32, kind="ExternalOutput").ap()
    elif dbg == "scr":
        dbg_out = dt("dbgo", [TOK, 2 * D], F32, kind="ExternalOutput").ap()
        dbg_out2 = dt("dbgo2", [4, 128, 16 * 512], F32, kind="ExternalOutput").ap()

    with ExitStack() as octx:
        P = Prog(nc, octx)
        build.last_prog = P
        block = octx.enter_context(nc.Block())
        sb = lambda *a: octx.enter_context(nc.sbuf_tensor(*a))

        x_sb = [sb(f"x{i}", [128, D], F32) for i in range(2)]
        sc1 = sb("sc1", [128, 24], F32)
        stat = sb("stat", [128, 4, 6], F32)
        sums = [sb(f"sums{i}", [128, 1], F32) for i in range(2)]
        negmx = [sb(f"negmx{i}", [128, 1], F32) for i in range(2)]
        rsum = [sb(f"rsum{i}", [128, 1], F32) for i in range(2)]
        idn_sb = sb("idn", [128, 128], BF16)

        m_v = sc1[:, 0:1]
        v_v = sc1[:, 1:2]
        vpe = sc1[:, 2:3]
        ivpe = sc1[:, 3:4]
        ms_ = sc1[:, 4:5]
        rstd = sc1[:, 5:6]
        rstd2 = sc1[:, 6:7]
        rr2 = [sc1[:, 7:8], sc1[:, 8:9]]
        tln = sc1[:, 9:10]
        tln2 = sc1[:, 10:11]
        c_eps5 = sc1[:, 11:12]
        c_eps11 = sc1[:, 12:13]

        bctx = ExitStack()
        sbB = lambda *a: bctx.enter_context(nc.sbuf_tensor(*a))
        xn_sb = [sbB(f"xn{i}", [128, D], BF16) for i in range(2)]
        xn2_sb = [sbB(f"xn2{i}", [128, D], BF16) for i in range(2)]
        t1_sb = sbB("t1_", [128, D], F32)
        t2_sb = sbB("t2_", [128, 512], F32)
        t3_sb = sbB("t3_", [128, 512], F32)
        xn2T = sbB("xn2T", [128, 16, 512], BF16)
        xnTst = sbB("xnTst", [128, 16, 256], BF16)
        lnwb = sbB("lnwb", [128, D], F32)
        wkv_sl = [sbB(f"wkv{i}", [128, 8, 512], BF16) for i in range(2)]
        wq_sl = [sbB(f"wqs{i}", [128, 16, 256], BF16) for i in range(2)]
        wo_sl = [sbB(f"wos{i}", [128, 16, 256], BF16) for i in range(2)]
        kT = sbB("kT", [128, 4, TOKP], BF16)
        v_sb = sbB("v_sb", [128, 12, 512], BF16)
        qa_sb = [sbB("qa0", [128, 16, 512], BF16),
                 sbB("qa1", [128, 16, 512], BF16)]
        qb_sb = sbB("qb", [128, 512], BF16)
        ropec = sbB("ropecb", [128, 512], F32)
        ropes = sbB("ropesb", [128, 512], F32)
        p_sb = [sbB(f"p{i}", [128, 640], BF16) for i in range(2)]
        pt_sb = [sbB(f"pt{i}", [128, 640], BF16) for i in range(2)]
        m0_sb = sbB("m0", [128, 4, 640], BF16)
        psw_sb = sbB("psw", [128, 128], BF16)
        srow = [sbB(f"srow{i}", [128, 256], F32) for i in range(2)]

        d_pro = P.sem("d_pro")
        d_pro2 = P.sem("d_pro2")
        d_x = [P.sem(f"d_x{i}") for i in range(2)]
        d_rope = P.sem("d_rope")
        d_wkv = [P.sem(f"d_wkv{i}") for i in range(2)]
        d_wq = [P.sem(f"d_wq{i}") for i in range(2)]
        d_wo = [P.sem(f"d_wo{i}") for i in range(2)]
        d_xnsp = P.sem("d_xnsp")
        d_mskf = P.sem("d_mskf")
        d_ssp = [P.sem(f"d_ssp{i}") for i in range(2)]
        sT = P.sem("sT")
        sV = P.sem("sV")
        sS = P.sem("sS")
        sG = P.sem("sG")
        f_x = [P.sem(f"f_x{i}") for i in range(2)]
        f_xg = [P.sem(f"f_xg{i}") for i in range(2)]
        xng = [0, 0]
        f_rope = P.sem("f_rope")
        f_wkv = [P.sem(f"f_wkv{i}") for i in range(2)]
        f_wq = [P.sem(f"f_wq{i}") for i in range(2)]
        f_wo = [P.sem(f"f_wo{i}") for i in range(2)]

        marks = {}

        def tbar():
            return [(sV, sV.n), (sS, sS.n)]

        P.op("sync", lambda s: s.dma_start(out=lnwb[:], in_=lnw_in),
             incs=[(d_pro, 16)])
        P.op("sync", lambda s: s.dma_start(
            out=m0_sb[:], in_=m0_in.rearrange("p (q w) -> p q w", w=640)),
            incs=[(d_pro, 16)])
        P.op("gpsimd", lambda g: g.dma_start(out=psw_sb[:], in_=psw_in),
             incs=[(d_pro2, 16)])
        P.op("gpsimd", lambda g: g.dma_start(out=idn_sb[:], in_=idn_in),
             incs=[(d_pro2, 16)])
        P.op("vector", lambda v: v.memset(c_eps5[:], 1e-5), incs=[(sV, 1)])
        P.op("vector", lambda v: v.memset(c_eps11[:], 1e-11), incs=[(sV, 1)])

        def load_x(ch):
            par = ch % 2
            P.op("sync", lambda s, ch=ch, par=par: s.dma_start(
                out=x_sb[par][:], in_=x_in[128 * ch:128 * (ch + 1), :]),
                waits=[(f_x[par], ch // 2)],
                incs=[(d_x[par], 16)])

        def load_rope(g):
            P.op("sync", lambda s, g=g: s.dma_start(
                out=ropec[:], in_=ropec_in[:, 512 * g:512 * (g + 1)]),
                waits=[(f_rope, g)], incs=[(d_rope, 16)])
            P.op("sync", lambda s, g=g: s.dma_start(
                out=ropes[:], in_=ropes_in[:, 512 * g:512 * (g + 1)]),
                incs=[(d_rope, 16)])

        wkv_idx = [0]

        def load_wkv(g):
            for (src, kh) in ((wk_in, 0), (wk_in, 1), (wv_in, 0), (wv_in, 1)):
                i = wkv_idx[0]
                par = i % 2
                P.op("gpsimd", lambda gg, src=src, kh=kh, par=par: gg.dma_start(
                    out=wkv_sl[par][:], in_=src[:, 8 * kh:8 * (kh + 1), :]),
                    waits=[(f_wkv[par], i // 2)], incs=[(d_wkv[par], 16)])
                wkv_idx[0] += 1

        wq_idx = [0]

        def load_wq(g):
            for j in range(8):
                i = wq_idx[0]
                par = i % 2
                P.op("gpsimd", lambda gg, j=j, par=par: gg.dma_start(
                    out=wq_sl[par][:], in_=wq_in[:, :, 256 * j:256 * (j + 1)]),
                    waits=[(f_wq[par], i // 2)], incs=[(d_wq[par], 16)])
                wq_idx[0] += 1

        wo_idx = [0]

        def load_wo(b):
            for j in range(8):
                i = wo_idx[0]
                par = i % 2
                P.op("gpsimd", lambda gg, j=j, par=par: gg.dma_start(
                    out=wo_sl[par][:], in_=wo_in[:, :, 256 * j:256 * (j + 1)]),
                    waits=[(f_wo[par], i // 2)], incs=[(d_wo[par], 16)])
                wo_idx[0] += 1

        srow_use = [0, 0]

        def gen_LNT(g, tr):
            for c in range(4):
                ch = 4 * g + c
                par = ch % 2
                loaded = (d_x[par], 16 * (ch // 2 + 1))
                for i in range(4):
                    P.op("vector", lambda v, par=par, i=i: v.bn_stats(
                        stat[:, i, :], x_sb[par][:, 512 * i:512 * (i + 1)]),
                        waits=[loaded] if i == 0 else [],
                        incs=[(sV, 1)] if i == 3 else [])
                    if i == 1:
                        yield
                P.op("vector", lambda v: v.bn_aggr(sc1[:, 0:2], stat[:]),
                     waits=[(sV, sV.n)], incs=[(sV, 1)])
                aggr_done = sV.n
                P.op("scalar", lambda s: s.activation(tln, v_v, AF.Ln, bias=c_eps5, scale=1.0),
                     waits=[(sS, sS.n), (sV, aggr_done)], incs=[(sS, 1)])
                P.op("scalar", lambda s: s.activation(rstd, tln, AF.Exp, bias=0.0, scale=-0.5),
                     waits=[(sS, sS.n)], incs=[(sS, 1)])
                rstd_done = sS.n
                P.op("scalar", lambda s: s.activation(
                    tln2, v_v, AF.Ln, bias=c_eps11, scale=1.0 + 1e-6),
                    waits=[(sS, sS.n)], incs=[(sS, 1)])
                P.op("scalar", lambda s, par=par: s.activation(
                    rr2[par], tln2, AF.Exp, bias=0.0, scale=-0.5),
                    waits=[(sS, sS.n)], incs=[(sS, 1)])
                rr2_done = sS.n
                yield
                if g >= 1:
                    ptx = marks.get(("txn", par))
                    for hf in range(2):
                        P.op("vector", lambda v, par=par, hf=hf: v.tensor_scalar(
                            xn_sb[par][:, 1024 * hf:1024 * (hf + 1)],
                            x_sb[par][:, 1024 * hf:1024 * (hf + 1)],
                            m_v, rstd, ALU.subtract, ALU.mult),
                            waits=([(sS, rstd_done)] +
                                   ([(sT, ptx)] if ptx is not None else []))
                                  if hf == 0 else [],
                            incs=[(sV, 1)])
                        yield
                xn_done = sV.n
                t1f = marks.get("t1free")
                for hf in range(2):
                    sttw = []
                    if hf == 0:
                        sttw = [(sS, rstd_done)] + \
                               ([(d_pro, 32)] if (g == 0 and c == 0) else []) + \
                               ([(sS, t1f)] if t1f is not None else [])
                    P.op("vector", lambda v, par=par, hf=hf: v.scalar_tensor_tensor(
                        t1_sb[:, 1024 * hf:1024 * (hf + 1)],
                        x_sb[par][:, 1024 * hf:1024 * (hf + 1)],
                        m_v, lnwb[:, 1024 * hf:1024 * (hf + 1)],
                        ALU.subtract, ALU.mult),
                        waits=sttw,
                        incs=[(sV, 1)] + ([(f_x[par], 1)] if hf == 1 else []))
                    if hf == 1:
                        stt_done = sV.n
                    yield
                if ch + 2 < NCH:
                    load_x(ch + 2)
                ptx2 = marks.get(("txn2", par))
                for hf in range(2):
                    P.op("scalar", lambda s, par=par, hf=hf: s.activation(
                        xn2_sb[par][:, 1024 * hf:1024 * (hf + 1)],
                        t1_sb[:, 1024 * hf:1024 * (hf + 1)],
                        AF.Copy, bias=0.0, scale=rr2[par]),
                        waits=([(sS, sS.n), (sV, stt_done)] +
                               ([(sT, ptx2)] if ptx2 is not None else []))
                              if hf == 0 else [],
                        incs=[(sS, 1)])
                    yield
                xn2_done = sS.n
                marks["t1free"] = xn2_done
                srcs = [(xn2_sb[par], xn2T, (sS, xn2_done))]
                if g >= 1:
                    srcs.append((xn_sb[par], xnTst, (sV, xn_done)))
                for (srcb, dst, dep) in srcs:
                    cb = 128 * c if dst is xn2T else 128 * (c % 2)
                    m = 4 * (g - 1) + c
                    for half in range(2):
                        for k in range(8):
                            kk = 8 * half + k
                            wl = []
                            if k == 0:
                                wl = [dep]
                                prev = marks.get("trev")
                                if prev is not None:
                                    wl.append((sV if prev[0] == "V" else sS, prev[1]))
                                if dst is xnTst and m >= 2:
                                    wl.append((d_xnsp, 16 * (2 * (g - 1) + c // 2)))
                                if c == 0 and half == 0 and dst is xn2T:
                                    wl += tbar()
                                    if g == 0:
                                        wl.append((d_pro2, 32))
                            P.op("tensor", lambda t, srcb=srcb, kk=kk, k=k: t.transpose(
                                tr[:, 128 * k:128 * (k + 1)],
                                srcb[:, 128 * kk:128 * (kk + 1)], idn_sb[:]),
                                waits=wl, incs=[(sT, 1)] if k == 7 else [])
                        tdone = sT.n
                        if half == 0:
                            P.op("vector", lambda v, dst=dst, half=half, cb=cb: v.tensor_copy(
                                dst[:, 8 * half:8 * (half + 1), cb:cb + 128],
                                tr.ap().rearrange("p (k t) -> p k t", t=128)),
                                waits=[(sV, sV.n), (sT, tdone)], incs=[(sV, 1)])
                            marks["trev"] = ("V", sV.n)
                        else:
                            P.op("scalar", lambda s, dst=dst, half=half, cb=cb: s.activation(
                                dst[:, 8 * half:8 * (half + 1), cb:cb + 128],
                                tr.ap().rearrange("p (k t) -> p k t", t=128),
                                AF.Copy, bias=0.0, scale=1.0),
                                waits=[(sS, sS.n), (sT, tdone)], incs=[(sS, 1)])
                            marks["trev"] = ("S", sS.n)
                        if dst is xnTst and half == 1 and c % 2 == 1:
                            P.op("sync", lambda s, g=g, c=c: s.dma_start(
                                out=xn_dram[2 * (g - 1) + c // 2, :, :],
                                in_=xnTst[:, :, :].rearrange("p k t -> p (k t)")),
                                waits=[(sV, sV.n), (sS, sS.n)], incs=[(d_xnsp, 16)])
                        yield
                    marks[("txn2" if dst is xn2T else "txn", par)] = sT.n
            marks[("lnt_done", g)] = (sV.n, sS.n)

        def phase_PROJ_kv(g, pj_ps, pm_ps):
            lnt_v, lnt_s = marks[("lnt_done", g)]
            base = wkv_idx[0] - 4
            par = g % 2
            rope_ok = (d_rope, 32 * (g + 1))
            for kc in range(4):
                pj = pj_ps[kc % 2]
                pm = pm_ps[kc % 2]
                for k in range(16):
                    sl = base + (k // 8)
                    wl = []
                    if k == 0:
                        wl = [(sV, lnt_v), (sS, lnt_s), (d_wkv[sl % 2], 16 * (sl // 2 + 1))]
                        prev = marks.get(("kv_free", kc % 2))
                        if prev is not None:
                            wl.append((sV, prev))
                        if kc == 0:
                            wl += tbar()
                    elif k == 8:
                        wl = [(d_wkv[sl % 2], 16 * (sl // 2 + 1))]
                    kinc = []
                    if k == 15:
                        kinc = [(sT, 1)]
                        if kc == 3:
                            kinc += [(f_wkv[0], 1), (f_wkv[1], 1)]
                    P.op("tensor", lambda t, pj=pj, kc=kc, k=k, sl=sl: t.matmul(
                        pj[:], wkv_sl[sl % 2][:, k % 8, 128 * kc:128 * (kc + 1)],
                        xn2T[:, k, :], start=(k == 0), stop=(k == 15)),
                        waits=wl, incs=kinc)
                kmm = sT.n
                P.op("scalar", lambda s, pj=pj, kc=kc: s.activation(
                    qb_sb[:], pj[:], AF.Copy, bias=0.0, scale=1.0),
                    waits=[(sS, sS.n), (sT, kmm)], incs=[(sS, 1)])
                kev = sS.n
                pmw = [(sS, kev), (d_pro2, 32)]
                prevkv = marks.get(("kv_free", kc % 2))
                if prevkv is not None:
                    pmw.append((sV, prevkv))
                P.op("tensor", lambda t, pm=pm, kc=kc: t.matmul(
                    pm[:], psw_sb[:], qb_sb[:], start=True, stop=True),
                    waits=pmw, incs=[(sT, 1)])
                kpm = sT.n
                P.op("vector", lambda v, pm=pm, kc=kc, par=par: v.tensor_tensor(
                    t2_sb[:], pm[:], ropes[:], ALU.mult),
                    waits=[(sV, sV.n), (sT, kpm), rope_ok])
                P.op("vector", lambda v, pj=pj, kc=kc, par=par: v.tensor_tensor(
                    t3_sb[:], pj[:], ropec[:], ALU.mult), incs=[(sV, 1)])
                P.op("vector", lambda v, kc=kc, g=g: v.tensor_tensor(
                    kT[:, kc, 512 * g:512 * (g + 1)], t2_sb[:], t3_sb[:],
                    ALU.add), waits=[(sV, sV.n)], incs=[(sV, 1)])
                marks[("kv_free", kc % 2)] = sV.n
            for c in range(4):
                pj = pj_ps[c % 2]
                for k in range(16):
                    sl = base + 2 + (k // 8)
                    wl = []
                    if k == 0:
                        wl = [(d_wkv[sl % 2], 16 * (sl // 2 + 1)),
                              (sV, marks[("kv_free", c % 2)])]
                        prev = marks.get(("v_free", c % 2))
                        if prev is not None:
                            wl.append((sS, prev))
                    elif k == 8:
                        wl = [(d_wkv[sl % 2], 16 * (sl // 2 + 1))]
                    vinc = []
                    if k == 15:
                        vinc = [(sT, 1)]
                        if c == 3:
                            vinc += [(f_wkv[0], 1), (f_wkv[1], 1)]
                    P.op("tensor", lambda t, pj=pj, c=c, k=k, sl=sl: t.matmul(
                        pj[:], xn2T[:, k, 128 * c:128 * (c + 1)],
                        wkv_sl[sl % 2][:, k % 8, :], start=(k == 0), stop=(k == 15)),
                        waits=wl, incs=vinc)
                vmm = sT.n
                P.op("scalar", lambda s, pj=pj, g=g, c=c: s.activation(
                    v_sb[:, (4 * g + c) % 12, :], pj[:], AF.Copy, bias=0.0, scale=1.0),
                    waits=[(sS, sS.n), (sT, vmm)], incs=[(sS, 1)])
                marks[("v_free", c % 2)] = sS.n
            marks[("projkv_done", g)] = sS.n
            if g == 0:
                P.op("vector", lambda v: v.nop(), waits=[(sV, sV.n)], incs=[(f_rope, 1)])

        def phase_PROJ_q(g, pj_ps, pm_ps):
            par = g % 2
            base = wq_idx[0] - 8
            rope_ok = (d_rope, 32 * (g + 1))
            qev = [0] * 16
            qpm = [0] * 16

            def stM(h):
                sl = base + h // 2
                hh = h % 2
                pj = pj_ps[h % 2]
                for k in range(16):
                    wl = []
                    if k == 0:
                        wl = [(d_wq[sl % 2], 16 * (sl // 2 + 1))]
                        prev = marks.get(("q_free", h % 2))
                        if prev is not None:
                            wl.append((sV, prev))
                        if h == 0:
                            wl.append((sS, marks[("projkv_done", g)]))
                    P.op("tensor", lambda t, pj=pj, sl=sl, hh=hh, k=k: t.matmul(
                        pj[:], wq_sl[sl % 2][:, k, 128 * hh:128 * (hh + 1)],
                        xn2T[:, k, :], start=(k == 0), stop=(k == 15)),
                        waits=wl,
                        incs=([(sT, 1)] + ([(f_wq[sl % 2], 1)] if hh == 1 else []))
                             if k == 15 else [])

            def stEv(h):
                pj = pj_ps[h % 2]
                P.op("scalar", lambda s, pj=pj, h=h: s.activation(
                    qb_sb[:], pj[:], AF.Copy, bias=0.0, scale=1.0),
                    waits=[(sS, sS.n), (sT, sT.n)], incs=[(sS, 1)])
                qev[h] = sS.n

            def stPm(h):
                pm = pm_ps[h % 2]
                pmw = [(sS, qev[h])]
                prevq = marks.get(("q_free", h % 2))
                if prevq is not None:
                    pmw.append((sV, prevq))
                P.op("tensor", lambda t, pm=pm, h=h: t.matmul(
                    pm[:], psw_sb[:], qb_sb[:], start=True, stop=True),
                    waits=pmw, incs=[(sT, 1)])
                qpm[h] = sT.n

            def stR(h):
                pm = pm_ps[h % 2]
                pj = pj_ps[h % 2]
                P.op("vector", lambda v, pm=pm, h=h: v.tensor_tensor(
                    t2_sb[:], pm[:], ropes[:], ALU.mult),
                    waits=[(sV, sV.n), (sT, qpm[h]), rope_ok])
                P.op("vector", lambda v, pj=pj, h=h: v.tensor_tensor(
                    t3_sb[:], pj[:], ropec[:], ALU.mult), incs=[(sV, 1)])
                P.op("vector", lambda v, h=h: v.tensor_tensor(
                    qa_sb[par][:, h, :], t2_sb[:], t3_sb[:], ALU.add),
                    waits=[(sV, sV.n)], incs=[(sV, 1)])
                marks[("q_free", h % 2)] = sV.n

            for h in range(17):
                if h >= 1:
                    stPm(h - 1)
                if h < 16:
                    stM(h)
                if h >= 1:
                    stR(h - 1)
                if h < 16:
                    stEv(h)
            P.op("vector", lambda v: v.nop(), waits=[(sV, sV.n)], incs=[(f_rope, 1)])
            marks[("projq_done", g)] = sV.n

        def phase_ATTN(b, sc_ps, pt_ps, pv_ps, pump=None, wop=None):
            d_pro_gate = 80 if (dbg == "kvq1" and b == 0) else 32
            qv = marks[("projq_done", b + 1)]
            qbuf = qa_sb[(b + 1) % 2]
            NT = 64
            mA = [0] * NT; mB = [0] * NT; mC = [0] * NT; mD = [0] * NT
            mE = [0] * NT; mF = [0] * NT; mG = [0] * NT; mH = [0] * 16

            def stA(i):
                h, qc = divmod(i, 4)
                hkv = h // 4
                par = i % 2
                sc = sc_ps[par]
                w0 = 512 * b + 128 * qc
                wl = []
                if i < 2:
                    wl.append((sV, qv))
                if i >= 2:
                    wl.append((sS, mC[i - 2]))
                if i == 0:
                    wl.append((d_pro, d_pro_gate))
                    wl += tbar()
                if b >= 1 and i == 0:
                    wl.append((d_mskf, 16))
                P.op("tensor", lambda t, sc=sc, h=h, hkv=hkv, qc=qc, w0=w0: t.matmul(
                    sc[:, 0:512], qbuf[:, h, 128 * qc:128 * (qc + 1)],
                    kT[:, hkv, w0:w0 + 512], start=True, stop=False), waits=wl)
                P.op("tensor", lambda t, sc=sc, h=h, hkv=hkv, qc=qc, w0=w0: t.matmul(
                    sc[:, 512:640], qbuf[:, h, 128 * qc:128 * (qc + 1)],
                    kT[:, hkv, w0 + 512:w0 + 640], start=True, stop=False))
                P.op("tensor", lambda t, sc=sc, qc=qc: t.matmul(
                    sc[:, 0:512], idn_sb[:], m0_sb[:, qc, 0:512],
                    start=False, stop=True))
                P.op("tensor", lambda t, sc=sc, qc=qc: t.matmul(
                    sc[:, 512:640], idn_sb[:], m0_sb[:, qc, 512:640],
                    start=False, stop=True), incs=[(sT, 1)])
                mA[i] = sT.n

            def stB(i):
                par = i % 2
                sc = sc_ps[par]
                P.op("vector", lambda v, sc=sc, par=par: v.tensor_reduce(
                    negmx[par][:], sc[:], axis=AX.X, op=ALU.max, negate=True),
                    waits=[(sV, sV.n), (sT, mA[i])], incs=[(sV, 1)])
                mB[i] = sV.n

            def stC(i):
                par = i % 2
                sc = sc_ps[par]
                wl = [(sS, sS.n), (sV, mB[i])]
                if i >= 2:
                    wl.append((sT, mE[i - 2]))
                P.op("scalar", lambda s, sc=sc, par=par: s.activation(
                    p_sb[par][:], sc[:], AF.Exp, bias=negmx[par][:, 0:1],
                    scale=1.0, accum_out=sums[par][:, 0:1]),
                    waits=wl, incs=[(sS, 1)])
                mC[i] = sS.n

            def stD(i):
                par = i % 2
                P.op("vector", lambda v, par=par: v.reciprocal(rsum[par][:], sums[par][:]),
                     waits=[(sV, sV.n), (sS, mC[i])], incs=[(sV, 1)])
                P.op("vector", lambda v, par=par: v.tensor_scalar(
                    p_sb[par][:], p_sb[par][:], rsum[par][:, 0:1], None, ALU.mult),
                    waits=[(sV, sV.n)], incs=[(sV, 1)])
                mD[i] = sV.n

            def stE(i):
                par = i % 2
                ptp = pt_ps[0]
                for c in range(5):
                    wl = []
                    if c == 0:
                        wl = [(sV, mD[i])]
                        if i >= 1:
                            wl.append((sS, mF[i - 1]))
                    P.op("tensor", lambda t, ptp=ptp, par=par, c=c: t.transpose(
                        ptp[:, 128 * c:128 * (c + 1)], p_sb[par][:, 128 * c:128 * (c + 1)],
                        idn_sb[:]), waits=wl, incs=[(sT, 1)] if c == 4 else [])
                mE[i] = sT.n

            def stF(i):
                par = i % 2
                ptp = pt_ps[0]
                wl = [(sS, sS.n), (sT, mE[i])]
                if i >= 2:
                    wl.append((sT, mG[i - 2]))
                P.op("scalar", lambda s, ptp=ptp, par=par: s.activation(
                    pt_sb[par][:], ptp[:], AF.Copy, bias=0.0, scale=1.0),
                    waits=wl, incs=[(sS, 1)])
                mF[i] = sS.n

            def stG(i):
                h, qc = divmod(i, 4)
                hkv = h // 4
                par = i % 2
                pv = pv_ps[0]
                for c in range(5):
                    tc = 4 * b + qc + c
                    wl = []
                    if c == 0:
                        wl = [(sS, mF[i])]
                        if qc == 0 and h >= 1:
                            wl.append((sS, mH[h - 1]))
                    P.op("tensor", lambda t, pv=pv, qc=qc, c=c, tc=tc % 12, hkv=hkv, par=par:
                         t.matmul(pv[:, 128 * qc:128 * (qc + 1)],
                                  v_sb[:, tc, 128 * hkv:128 * (hkv + 1)],
                                  pt_sb[par][:, 128 * c:128 * (c + 1)],
                                  start=(c == 0), stop=(c == 4)),
                         waits=wl, incs=[(sT, 1)] if c == 4 else [])
                mG[i] = sT.n

            def stH(h):
                pv = pv_ps[0]
                P.op("scalar", lambda s, pv=pv, h=h: s.activation(
                    qbuf[:, h, :], pv[:], AF.Copy, bias=0.0, scale=1.0),
                    waits=[(sS, sS.n), (sT, mG[4 * h + 3])], incs=[(sS, 1)])
                mH[h] = sS.n

            for step in range(NT + 3):
                if step < NT:
                    stA(step)
                    stB(step)
                j = step - 1
                if 0 <= j < NT:
                    stC(j)
                    stD(j)
                k = step - 2
                if 0 <= k < NT:
                    stE(k)
                    stF(k)
                m = step - 3
                if 0 <= m < NT:
                    stG(m)
                    if m % 4 == 3:
                        stH(m // 4)
                if wop is not None and step % 2 == 1:
                    next(wop, None)
                if pump is not None:
                    next(pump, None)
            marks[("attn_done", b)] = sS.n

        def gen_WO(b, wo_ps, base, drain=False):
            av = marks[("attn_done", b)]
            qbuf = qa_sb[(b + 1) % 2]
            til = 0
            for j in range(8):
                sl = base + j
                for t128 in range(4):
                    par = til % 2
                    pj = wo_ps[til % len(wo_ps)]
                    for kq in range(16):
                        wl = []
                        if kq == 0:
                            wl = [(d_wo[sl % 2], 16 * (sl // 2 + 1)), (sS, av)]
                            prev = marks.get(("wo_free", til % len(wo_ps)))
                            if prev is not None:
                                wl.append((sV, prev))
                            if til == 0 and drain:
                                wl += tbar()
                        P.op("tensor", lambda t, pj=pj, sl=sl, t128=t128, kq=kq: t.matmul(
                            pj[:], qbuf[:, kq, 128 * t128:128 * (t128 + 1)],
                            wo_sl[sl % 2][:, kq, :], start=(kq == 0), stop=(kq == 15)),
                            waits=wl,
                            incs=([(sT, 1)] + ([(f_wo[sl % 2], 1)] if t128 == 3 else []))
                                 if kq == 15 else [])
                    womm = sT.n
                    P.op("vector", lambda v, pj=pj, par=par: v.tensor_copy(
                        srow[par][:], pj[:]),
                        waits=[(sV, sV.n), (sT, womm), (d_ssp[par], 16 * srow_use[par])],
                        incs=[(sV, 1)])
                    marks[("wo_free", til % len(wo_ps))] = sV.n
                    evd = sV.n
                    P.op("sync", lambda s, b=b, j=j, t128=t128, par=par: s.dma_start(
                        out=s_dram[512 * b + 128 * t128:512 * b + 128 * (t128 + 1),
                                   256 * j:256 * (j + 1)], in_=srow[par][:]),
                        waits=[(sV, evd)], incs=[(d_ssp[par], 16)])
                    srow_use[par] += 1
                    til += 1
                    yield
            marks[("wo_done", b)] = sV.n

        def emit_kvq_dump():
            P.op("gpsimd", lambda s: s.dma_start(
                out=dbg_out[:, 0:4 * TOKP].rearrange("p (a b) -> p a b", b=TOKP),
                in_=kT[:, :, :]),
                waits=[(dmk["sem"], dmk["val"])], incs=[(d_pro, 16)])
            P.op("gpsimd", lambda s: s.dma_start(
                out=dbg_out[:, 4 * TOKP:4 * TOKP + 12 * 512].rearrange(
                    "p (a b) -> p a b", b=512), in_=v_sb[:, :, :]),
                incs=[(d_pro, 16)])
            P.op("gpsimd", lambda s: s.dma_start(
                out=dbg_out[:, 4 * TOKP + 20 * 512:].rearrange(
                    "p (a b) -> p a b", b=512), in_=qa_sb[1][:, :, :]),
                incs=[(d_pro, 16)])
        dmk = {"sem": P.sem("dbggate"), "val": 1}
        load_x(0)
        load_x(1)
        load_rope(0)
        with ExitStack() as pctx:
            pp = lambda *a: pctx.enter_context(nc.psum_tensor(*a))
            tr0 = pp("tr_pro0", [128, 1024], BF16)
            for _ in gen_LNT(0, tr0):
                pass
            P.flush(block)
        load_wkv(0)
        with ExitStack() as pctx:
            pp = lambda *a: pctx.enter_context(nc.psum_tensor(*a))
            pj_ps = [pp("pj0_0", [128, 512], F32), pp("pj0_1", [128, 512], F32)]
            pm_ps = [pp("pm0_0", [128, 512], F32), pp("pm0_1", [128, 512], F32)]
            phase_PROJ_kv(0, pj_ps, pm_ps)
            P.flush(block)
        with ExitStack() as pctx:
            pp = lambda *a: pctx.enter_context(nc.psum_tensor(*a))
            tr1 = pp("tr_pro1", [128, 1024], BF16)
            for _ in gen_LNT(1, tr1):
                pass
            P.flush(block)
        for g in range(1, NG):
            load_rope(g)
            load_wkv(g)
            load_wq(g)
            if g >= 2:
                load_wo(g - 2)
            if g == NG - 1:
                load_wo(g - 1)
            if dbg == "kvq1" and g == 1:
                emit_kvq_dump()
            with ExitStack() as pctx:
                pp = lambda *a: pctx.enter_context(nc.psum_tensor(*a))
                pj_ps = [pp(f"pj{g}_{i}", [128, 512], F32) for i in range(2)]
                pm_ps = [pp(f"pm{g}_{i}", [128, 512], F32) for i in range(2)]
                phase_PROJ_kv(g, pj_ps, pm_ps)
                phase_PROJ_q(g, pj_ps, pm_ps)
                P.flush(block)
            with ExitStack() as pctx:
                pp = lambda *a: pctx.enter_context(nc.psum_tensor(*a))
                sc_ps = [pp(f"sc{g}_{i}", [128, 640], F32) for i in range(2)]
                pt_ps = [pp(f"ptp{g}_0", [128, 640], BF16)]
                pv_ps = [pp(f"pv{g}_0", [128, 512], F32)]
                trg = pp(f"trg{g}", [128, 1024], BF16)
                wo_ps = pp(f"wop{g}", [128, 256], F32)
                pump = gen_LNT(g + 1, trg) if g + 1 < NG else None
                wop = gen_WO(g - 2, [wo_ps], 8 * (g - 2)) if g >= 2 else None
                phase_ATTN(g - 1, sc_ps, pt_ps, pv_ps, pump=pump, wop=wop)
                if wop is not None:
                    for _ in wop:
                        pass
                if pump is not None:
                    for _ in pump:
                        pass
                P.flush(block)
            if g == 1:
                P.op("sync", lambda s: s.dma_start(
                    out=m0_sb[:], in_=mkf_in.rearrange("p (q w) -> p q w", w=640)),
                    waits=[(sS, marks[("attn_done", 0)])], incs=[(d_mskf, 16)])
        with ExitStack() as pctx:
            pp = lambda *a: pctx.enter_context(nc.psum_tensor(*a))
            wo_ps2 = [pp(f"wopF{i}", [128, 256], F32) for i in range(2)]
            for _ in gen_WO(NG - 2, wo_ps2, 8 * (NG - 2), drain=True):
                pass
            P.flush(block)
        P.flush(block)
        bctx.close()

        cctx = ExitStack()
        sbC = lambda *a: cctx.enter_context(nc.sbuf_tensor(*a))
        xnT_sl = sbC("xnT", [128, 16, 512], BF16)
        xc_sb = [sbC(f"xc{i}", [128, D], F32) for i in range(2)]
        wg_sl = [sbC(f"wg{i}", [128, 16, 512], BF16) for i in range(2)]
        wu_sl = [sbC(f"wu{i}", [128, 16, 512], BF16) for i in range(2)]
        wd_sl = [sbC(f"wd{i}", [128, 1024], BF16) for i in range(4)]
        fuse = sbC("fuse", [128, 64, 512], BF16)
        sg_sb = sbC("sgc", [128, 64], F32)
        su_sb = sbC("suc", [128, 64], F32)
        sdb_sb = sbC("sdbc", [128, D], F32)
        s_rl = [sbC(f"srl{i}", [128, 512], F32) for i in range(2)]
        epi1 = [sbC(f"epi1{i}", [128, 512], F32) for i in range(2)]
        epi2 = [sbC("epi20", [128, 512], F32)]
        y_sb = [sbC(f"y{i}", [128, 512], F32) for i in range(2)]

        d_cpro = P.sem("d_cpro")
        d_xnT = P.sem("d_xnT")
        d_wg = [P.sem(f"d_wg{i}") for i in range(2)]
        d_wu = [P.sem(f"d_wu{i}") for i in range(2)]
        d_wd = [P.sem(f"d_wd{i}") for i in range(4)]
        d_srl = [P.sem(f"d_srl{i}") for i in range(2)]
        d_y = [P.sem(f"d_y{i}") for i in range(2)]
        d_xr = [P.sem(f"d_xr{i}") for i in range(4)]
        f_xnT = P.sem("f_xnT")
        f_wg = [P.sem(f"f_wg{i}") for i in range(2)]
        f_wu = [P.sem(f"f_wu{i}") for i in range(2)]
        f_wd = [P.sem(f"f_wd{i}") for i in range(4)]
        f_srl = [P.sem(f"f_srl{i}") for i in range(2)]
        f_xr = [P.sem(f"f_xr{i}") for i in range(4)]

        b_ssp = (d_ssp[0].n, d_ssp[1].n)
        xslots = [x_sb[0], x_sb[1], xc_sb[0], xc_sb[1]]
        BBAR = [(sT, sT.n), (sV, sV.n), (sS, sS.n), (d_xnsp, 16 * 8),
                (d_ssp[0], b_ssp[0]), (d_ssp[1], b_ssp[1])]

        P.op("sync", lambda s: s.dma_start(out=sg_sb[:], in_=sg_in),
             waits=BBAR, incs=[(d_cpro, 16)])
        P.op("sync", lambda s: s.dma_start(out=su_sb[:], in_=su_in),
             incs=[(d_cpro, 16)])
        P.op("sync", lambda s: s.dma_start(out=sdb_sb[:], in_=sdb_in),
             incs=[(d_cpro, 16)])

        def load_xnT(tch):
            for hg in range(2):
                P.op("sync", lambda s, tch=tch, hg=hg: s.dma_start(
                    out=xnT_sl[:, :, 256 * hg:256 * (hg + 1)],
                    in_=xn_dram[2 * tch + hg].rearrange("p (k t) -> p k t", t=256)),
                    waits=([(f_xnT, tch), (d_xnsp, 16 * 2 * (tch + 1))] +
                           (BBAR if tch == 0 else [])) if hg == 0 else [],
                    incs=[(d_xnT, 16)])

        wgu_idx = [0]

        def load_wgu(j):
            i = wgu_idx[0]
            par = i % 2
            for (sem_d, sem_f, dst, srcw) in ((d_wg, f_wg, wg_sl, wg_in),
                                              (d_wu, f_wu, wu_sl, wu_in)):
                P.op("gpsimd", lambda gg, j=j, par=par, dst=dst, srcw=srcw: gg.dma_start(
                    out=dst[par][:], in_=srcw[:, :, 512 * j:512 * (j + 1)]),
                    waits=[(sem_f[par], i // 2)] + (BBAR if i < 2 else []),
                    incs=[(sem_d[par], 16)])
            wgu_idx[0] += 1

        wd_idx = [0]

        def load_wd():
            st = wd_idx[0]
            par = st % 4
            i, half = st % 64, (st // 64) % 2
            P.op("gpsimd", lambda gg, i=i, half=half, par=par: gg.dma_start(
                out=wd_sl[par][:], in_=wd_in[i, :, 1024 * half:1024 * (half + 1)]),
                waits=[(f_wd[par], st // 4)] + (BBAR if st < 4 else []),
                incs=[(d_wd[par], 16)])
            wd_idx[0] += 1

        srl_idx = [0]

        def load_s(t128g, ds):
            u = srl_idx[0]
            par = u % 2
            P.op("sync", lambda s, t128g=t128g, ds=ds, par=par: s.dma_start(
                out=s_rl[par][:], in_=s_dram[128 * t128g:128 * (t128g + 1),
                                             512 * ds:512 * (ds + 1)]),
                waits=[(f_srl[par], u // 2)] + (BBAR if u < 2 else []),
                incs=[(d_srl[par], 16)])
            srl_idx[0] += 1
            return u

        xr_uses = [0, 0, 0, 0]

        def load_xres(t128g):
            s4 = t128g % 4
            first = [(f_x[s4], 10)] if (s4 < 2 and xr_uses[s4] == 0) else []
            P.op("sync", lambda s, t128g=t128g, s4=s4: s.dma_start(
                out=xslots[s4][:], in_=x_in[HALO + 128 * t128g:HALO + 128 * (t128g + 1), :]),
                waits=first + [(f_xr[s4], max(0, xr_uses[s4]))] +
                      (BBAR if xr_uses[s4] == 0 else []),
                incs=[(d_xr[s4], 16)])
            xr_uses[s4] += 1

        y_use = [0, 0]
        load_xnT(0)
        load_wgu(0)
        load_wgu(1)
        for _ in range(4):
            load_wd()
        sorder = []
        for tch in range(4):
            for pr in range(2):
                for t in range(4):
                    for dsl in range(2):
                        sorder.append((4 * tch + t, 2 * pr + dsl))
        s_loaded = [0]
        for _ in range(2):
            load_s(*sorder[s_loaded[0]])
            s_loaded[0] += 1
        for t128g in range(4):
            load_xres(t128g)

        with ExitStack() as cpctx:
            pp = lambda *a: cpctx.enter_context(nc.psum_tensor(*a))
            gu_ps = [[pp("gu00", [128, 512], F32), pp("gu01", [128, 512], F32)],
                     [pp("gu10", [128, 512], F32), pp("gu11", [128, 512], F32)]]
            dn_ps = [pp(f"dnp{q}", [128, 512], F32) for q in range(4)]
            guflat = [gu_ps[0][0], gu_ps[0][1], gu_ps[1][0], gu_ps[1][1]]

            for tch in range(4):
                xnT_ok = (d_xnT, 32 * (tch + 1))
                for i in range(64):
                    sl = 16 * tch + (i // 4)
                    ip = i % 2
                    gps, ups = gu_ps[ip]
                    if i % 4 == 0 and sl + 2 < 64:
                        load_wgu((sl + 2) % 16)
                    for k in range(16):
                        wl = []
                        if k == 0:
                            wl = [(d_wg[sl % 2], 16 * (sl // 2 + 1)), xnT_ok]
                            prev = marks.get(("fuse_done", ip))
                            if prev is not None:
                                wl.append((sV, prev))
                            if i == 0 and tch == 0:
                                wl += tbar()
                            if i < 2 and tch > 0:
                                for pb8 in (4 + 2 * ip, 5 + 2 * ip):
                                    pm = marks.get(("epi_t1b", pb8))
                                    if pm is not None:
                                        wl.append((sV, pm))
                        P.op("tensor", lambda t, gps=gps, sl=sl, i=i, k=k: t.matmul(
                            gps[:], wg_sl[sl % 2][:, k, 128 * (i % 4):128 * (i % 4 + 1)],
                            xnT_sl[:, k, :], start=(k == 0), stop=(k == 15)),
                            waits=wl, incs=[(sT, 1)] if k == 15 else [])
                    for k in range(16):
                        wl = [(d_wu[sl % 2], 16 * (sl // 2 + 1))] if k == 0 else []
                        uinc = []
                        if k == 15:
                            uinc = [(sT, 1)]
                            if i % 4 == 3:
                                uinc += [(f_wg[sl % 2], 1), (f_wu[sl % 2], 1)]
                            if i == 63:
                                uinc += [(f_xnT, 1)]
                        P.op("tensor", lambda t, ups=ups, sl=sl, i=i, k=k: t.matmul(
                            ups[:], wu_sl[sl % 2][:, k, 128 * (i % 4):128 * (i % 4 + 1)],
                            xnT_sl[:, k, :], start=(k == 0), stop=(k == 15)),
                            waits=wl, incs=uinc)
                    gumm = sT.n
                    P.op("scalar", lambda s, gps=gps, i=i: s.activation(
                        fuse[:, i, :], gps[:], AF.Silu, bias=0.0, scale=sg_sb[:, i:i + 1]),
                        waits=[(sS, sS.n), (sT, gumm), (d_cpro, 48)], incs=[(sS, 1)])
                    sld = sS.n
                    P.op("vector", lambda v, ups=ups, i=i: v.scalar_tensor_tensor(
                        fuse[:, i, :], ups[:], su_sb[:, i:i + 1], fuse[:, i, :],
                        ALU.mult, ALU.mult),
                        waits=[(sV, sV.n), (sS, sld)], incs=[(sV, 1)])
                    marks[("fuse_done", ip)] = sV.n
                fuse_all = sV.n
                if tch + 1 < 4:
                    load_xnT(tch + 1)

                for pr in range(2):
                    B8 = dn_ps + guflat
                    for i in range(64):
                        st = (2 * tch + pr) * 64 + i
                        if st + 4 < 512:
                            load_wd()
                        for t in range(4):
                            for dsl in range(2):
                                b8 = 2 * t + dsl
                                wl = []
                                if t == 0 and dsl == 0:
                                    wl = [(d_wd[st % 4], 16 * (st // 4 + 1))]
                                    if i == 0:
                                        wl.append((sV, fuse_all))
                                        if tch == 0 and pr == 0:
                                            wl += tbar()
                                if i == 0:
                                    pb = marks.get(("epi_t1b", b8))
                                    if pb is not None:
                                        wl.append((sV, pb))
                                winc = []
                                if b8 == 7:
                                    winc = [(f_wd[st % 4], 1)]
                                    if i == 63:
                                        winc = [(sT, 1), (f_wd[st % 4], 1)]
                                P.op("tensor", lambda tt, b8=b8, st=st, i=i, t=t, dsl=dsl:
                                     tt.matmul(B8[b8][:], fuse[:, i, 128 * t:128 * (t + 1)],
                                               wd_sl[st % 4][:, 512 * dsl:512 * (dsl + 1)],
                                               start=(i == 0), stop=(i == 63)),
                                     waits=wl, incs=winc)
                    dnd = sT.n
                    tmk = {}
                    gmk = {}
                    t2mk = {}

                    def em_T1(u):
                        t, dsl = divmod(u, 2)
                        ds = 2 * pr + dsl
                        b8 = 2 * t + dsl
                        P.op("vector", lambda v, b8=b8, ds=ds, u=u: v.tensor_tensor(
                            epi1[u % 2][:], B8[b8][:],
                            sdb_sb[:, 512 * ds:512 * (ds + 1)], ALU.mult),
                            waits=[(sV, sV.n), (sT, dnd), (d_cpro, 48)],
                            incs=[(sV, 1)])
                        tmk[u] = sV.n
                        marks[("epi_t1b", b8)] = sV.n

                    def em_silu(u):
                        wl = [(sS, sS.n), (sV, tmk[u])]
                        if u >= 1:
                            wl.append((sV, t2mk[u - 1]))
                        P.op("scalar", lambda s, u=u: s.activation(
                            epi2[0][:], epi1[u % 2][:], AF.Silu,
                            bias=0.0, scale=1.0), waits=wl, incs=[(sS, 1)])
                        gmk[u] = sS.n

                    def em_T2Y(u):
                        t, dsl = divmod(u, 2)
                        t128g = 4 * tch + t
                        s4 = t128g % 4
                        ds = 2 * pr + dsl
                        uu = 16 * tch + 8 * pr + 2 * t + dsl
                        sp = uu % 2
                        P.op("vector", lambda v, sp=sp, u=u: v.tensor_tensor(
                            epi1[u % 2][:], epi2[0][:], s_rl[sp][:], ALU.mult),
                            waits=[(sV, sV.n), (sS, gmk[u]),
                                   (d_srl[sp], 16 * (uu // 2 + 1))],
                            incs=[(sV, 1), (f_srl[sp], 1)])
                        t2mk[u] = sV.n
                        if s_loaded[0] < 64:
                            load_s(*sorder[s_loaded[0]])
                            s_loaded[0] += 1
                        ypar = uu % 2
                        P.op("vector", lambda v, ypar=ypar, ds=ds, s4=s4, u=u,
                             t128g=t128g: v.tensor_tensor(
                                 y_sb[ypar][:], epi1[u % 2][:],
                                 xslots[s4][:, 512 * ds:512 * (ds + 1)], ALU.add),
                             waits=[(sV, sV.n), (d_xr[s4], 16 * ((t128g // 4) + 1)),
                                    (d_y[ypar], 16 * y_use[ypar])],
                             incs=[(sV, 1)] + ([(f_xr[s4], 1)] if ds == 3 else []))
                        yd = sV.n
                        P.op("sync", lambda s, t128g=t128g, ds=ds, ypar=ypar: s.dma_start(
                            out=y_out[128 * t128g:128 * (t128g + 1),
                                      512 * ds:512 * (ds + 1)], in_=y_sb[ypar][:]),
                            waits=[(sV, yd)], incs=[(d_y[ypar], 16)])
                        y_use[ypar] += 1
                        if ds == 3 and t128g + 4 < 16:
                            load_xres(t128g + 4)

                    for u in range(9):
                        if u < 8:
                            em_T1(u)
                        if u >= 1:
                            em_T2Y(u - 1)
                        if u < 8:
                            em_silu(u)
                    marks[("pass_done", pr)] = sV.n
            if dbg == "scr":
                P.op("sync", lambda s: s.dma_start(
                    out=dbg_out[:, 0:D], in_=s_dram[:, :]),
                    waits=[(d_y[0], 16 * y_use[0]), (d_y[1], 16 * y_use[1])],
                    incs=[(d_pro, 16)])
                P.op("gpsimd", lambda g: g.dma_start(
                    out=dbg_out2[:, :, :], in_=xn_dram[:, :, :]),
                    incs=[(d_pro2, 16)])
                P.op("sync", lambda s: s.nop(),
                     waits=[(d_pro, 16 * 4), (d_pro2, 16 * 3)])
            def fin(s):
                s.wait_ge(d_y[0].h, 16 * y_use[0])
                s.wait_ge(d_y[1].h, 16 * y_use[1])
            P.q["sync"].append(fin)
            P.flush(block)
        cctx.close()

    return nc



def _stage(inputs):
    x = np.asarray(inputs["L0_attn_x"]).reshape(SEQ, D).astype(np.float32)
    ln_w = np.asarray(inputs["L0_attn_ln_w"], np.float32)
    wq = np.asarray(inputs["L0_attn_wq"], np.float32)
    wk = np.asarray(inputs["L0_attn_wk"], np.float32)
    wv = np.asarray(inputs["L0_attn_wv"], np.float32)
    wo = np.asarray(inputs["L0_attn_wo"], np.float32)
    wg = np.asarray(inputs["L0_ffn_w_gate"], np.int8)
    wu = np.asarray(inputs["L0_ffn_w_up"], np.int8)
    wd = np.asarray(inputs["L0_ffn_w_down"], np.int8)
    sg = np.asarray(inputs["L0_ffn_s_gate"], np.float32)
    su = np.asarray(inputs["L0_ffn_s_up"], np.float32)
    sd = np.asarray(inputs["L0_ffn_s_down"], np.float32)
    bf = ml_dtypes.bfloat16

    wq_st = np.ascontiguousarray(wq.reshape(16, 128, D).transpose(1, 0, 2))
    wk_st = np.ascontiguousarray(wk.reshape(16, 128, 512).transpose(1, 0, 2))
    wv_st = np.ascontiguousarray(wv.reshape(16, 128, 512).transpose(1, 0, 2))
    wo_st = np.ascontiguousarray(wo.reshape(16, 128, D).transpose(1, 0, 2))
    wg_st = np.ascontiguousarray(wg.reshape(16, 128, INTER).transpose(1, 0, 2))
    wu_st = np.ascontiguousarray(wu.reshape(16, 128, INTER).transpose(1, 0, 2))
    wd_st = np.ascontiguousarray(wd.reshape(64, 128, D))
    sg_st = np.ascontiguousarray(sg.reshape(64, 128).T)
    su_st = np.ascontiguousarray(su.reshape(64, 128).T)
    sdb_st = np.ascontiguousarray(np.broadcast_to(sd, (128, D)))
    lnw_st = np.ascontiguousarray(np.broadcast_to(ln_w, (128, D)))

    qi = np.arange(128)[:, None]

    inv = 1.0 / 10000.0 ** (np.arange(0, HD, 2, dtype=np.float64) / HD)
    scale = 128.0 ** (-0.25)
    psw = np.zeros((128, 128), np.float32)
    for i in range(64):
        psw[2 * i + 1, 2 * i] = 1.0
        psw[2 * i, 2 * i + 1] = 1.0
    ident = np.eye(128, dtype=np.float32)

    wcol = np.arange(640)[None, :]
    band = np.where((wcol >= qi) & (wcol <= qi + 512), 0.0, NEG).astype(np.float32)
    mkf = np.broadcast_to(band[:, None, :], (128, 4, 640))
    mkf_st = np.ascontiguousarray(mkf.reshape(128, 4 * 640).astype(bf))

    in_maps = []
    for c in range(NCORES):
        lo = TOK * c - HALO
        xs = np.zeros((TOKP, D), np.float32)
        src_lo = max(lo, 0)
        xs[src_lo - lo:, :] = x[src_lo:TOK * (c + 1), :]
        pos = np.arange(lo, TOK * (c + 1), dtype=np.float64)
        ang = pos[None, :] * inv[:, None]
        cosv = np.cos(ang) * scale
        sinv = np.sin(ang) * scale
        C = np.empty((128, TOKP), np.float32)
        Sx = np.empty((128, TOKP), np.float32)
        C[0::2] = cosv
        C[1::2] = cosv
        Sx[0::2] = -sinv
        Sx[1::2] = sinv
        m0 = np.zeros((128, 4, 640), np.float32)
        for qc in range(4):
            m0[:, qc, :] = band
            if c == 0:
                m0[:, qc, :] = np.where(wcol < 512 - 128 * qc, NEG, m0[:, qc, :])
        in_maps.append(dict(
            x=xs, wq=wq_st, wk=wk_st, wv=wv_st, wo=wo_st,
            wg=wg_st, wu=wu_st, wd=wd_st, sg=sg_st, su=su_st,
            sdb=sdb_st, lnw=lnw_st,
            ropec=np.ascontiguousarray(C), ropes=np.ascontiguousarray(Sx),
            maskf=mkf_st,
            mask0=np.ascontiguousarray(m0.reshape(128, 4 * 640).astype(bf)),
            pswap=psw, ident=ident,
        ))
    return in_maps


def kernel(**inputs):
    in_maps = _stage(inputs)
    nc = build()
    res = run_bass_kernel_spmd(nc, in_maps, core_ids=list(range(NCORES)))
    y = np.concatenate([np.asarray(r["y"], np.float32) for r in res.results], axis=0)
    return y.reshape(1, SEQ, D)
